# version 04 (final)
# baseline (speedup 1.0000x reference)
import numpy as np
import ml_dtypes
from contextlib import ExitStack
import concourse.bass as bass
import concourse.mybir as mybir
from concourse.bass_utils import run_bass_kernel_spmd
from bass_rust import AP as RawAP

F32 = mybir.dt.float32
BF16 = mybir.dt.bfloat16
AF = mybir.ActivationFunctionType
ALU = mybir.AluOpType
AX = mybir.AxisListType
BFD = ml_dtypes.bfloat16

NEG = -1.0e30
NT = 16
DT = 8
SROW = 512


class Eng:
    def __init__(self, sem, step=1):
        self.sem = sem
        self.step = step
        self.n = 0

    def op(self, inst):
        inst.then_inc(self.sem, self.step)
        self.n += 1
        return self.n

    @property
    def val(self):
        return self.n * self.step


def build(nc):
    din = lambda name, shape, dt=F32: nc.dram_tensor(name, shape, dt, kind="ExternalInput").ap()
    x_d = din("xT", [1024, 2048])
    wqT_d = din("wqT", [1024, 1024])
    wkT_d = din("wkT", [1024, 256])
    wvT_d = din("wvT", [1024, 256])
    woT_d = din("woT", [1024, 1024])
    qnw_d = din("qnw", [128, 1])
    knw_d = din("knw", [128, 1])
    wqkvT_d = din("wqkvT", [1024, 3072])
    woutT_d = din("woutT", [1024, 1024])
    evT_d = din("evT", [64, 256])
    wfT_d = din("wfT", [1024, 1024])
    identb_d = din("identb", [128, 128], BF16)
    triu_d = din("triu", [128, 128], BF16)
    tril_d = din("tril", [128, 128], BF16)
    negrow_d = din("negrow", [128, 128], BF16)
    ones1_d = din("ones1", [128, 1], BF16)
    ones2_d = din("ones2", [128, 2], BF16)
    sel2_d = din("sel2", [2, 128])
    onesr_d = din("onesr", [1, 128])
    onesc_d = din("onesc", [128, 1])
    cvec_d = din("cvec", [128, 8])

    out_d = nc.dram_tensor("out", [1024, 2048], F32, kind="ExternalOutput").ap()
    S_d = nc.dram_tensor("Sscr", [16, 2048, SROW], BF16, kind="Internal").ap()
    Qh_d = nc.dram_tensor("Qh", [NT, 64, 2048], BF16, kind="Internal").ap()
    Kh_d = nc.dram_tensor("Kh", [NT, 64, 2048], BF16, kind="Internal").ap()
    Vh_d = nc.dram_tensor("Vh", [NT, 128, 1024], BF16, kind="Internal").ap()

    es = ExitStack()
    with es:
        sb = lambda name, shape, dt: es.enter_context(nc.sbuf_tensor(name, shape, dt))
        xd = [sb(f"xd{i}", [128, 2048], F32) for i in range(DT)]
        SCR = sb("SCR", [128, 4096], F32)
        SCRB = sb("SCRB", [128, 1024], BF16)
        identb = sb("s_identb", [128, 128], BF16)
        triu = sb("s_triu", [128, 128], BF16)
        tril = sb("s_tril", [128, 128], BF16)
        negrow = sb("s_negrow", [128, 128], BF16)
        ones1 = sb("s_ones1", [128, 1], BF16)
        ones2 = sb("s_ones2", [128, 2], BF16)
        sel2 = sb("s_sel2", [2, 128], F32)
        onesr = sb("s_onesr", [1, 128], F32)
        onesc = sb("s_onesc", [128, 1], F32)
        qnw = sb("s_qnw", [128, 1], F32)
        knw = sb("s_knw", [128, 1], F32)
        qnw8 = sb("qnw8", [128, 1], F32)
        evtb = sb("s_evtb", [64, 256], BF16)
        colA = sb("colA", [128, 8], F32)
        cvec = sb("s_cvec", [128, 8], F32)
        nmax = sb("nmax", [128, 16], F32)
        ztab = sb("ztab", [128, 16], F32)
        rztab = sb("rztab", [128, 16], F32)
        diagt = sb("diagt", [128, 128], BF16)
        expP = sb("expP", [128, 640], BF16)
        expTb = sb("expTb", [128, 640], BF16)
        YTb = [sb(f"YTb{i}", [128, 1024], BF16) for i in range(2)]
        YTodd = sb("YTodd", [64, 1024], BF16)

        PA = es.enter_context(nc.psum_tensor("PA", [128, 1024], F32))
        PB = es.enter_context(nc.psum_tensor("PB", [128, 1024], F32))
        PC = es.enter_context(nc.psum_tensor("PC", [128, 1024], F32))
        PD = es.enter_context(nc.psum_tensor("PD", [128, 512], F32))
        PE_ = es.enter_context(nc.psum_tensor("PE", [128, 512], F32))

        sem = lambda name: es.enter_context(nc.semaphore(name))
        dq, dg, st_, sv_, ss_ = sem("dq"), sem("dg"), sem("st"), sem("sv"), sem("ss")
        block = es.enter_context(nc.Block())

        SY = Eng(dq, 16)
        G = Eng(dg, 16)
        T = Eng(st_)
        V = Eng(sv_)
        S_ = Eng(ss_)

        @block.sync
        def _(sync):
            sync.wait_ge(dq, SY.val)
            for i in range(DT):
                SY.op(sync.dma_start(out=xd[i][:], in_=x_d[i * 128:(i + 1) * 128, :]))
            for t, d in [(identb, identb_d), (triu, triu_d), (tril, tril_d),
                         (negrow, negrow_d), (ones1, ones1_d), (ones2, ones2_d),
                         (sel2, sel2_d), (onesr, onesr_d), (onesc, onesc_d),
                         (qnw, qnw_d), (knw, knw_d), (cvec, cvec_d)]:
                SY.op(sync.dma_start(out=t[:], in_=d[:]))
            SY.op(sync.dma_start(out=SCR[0:64, 0:256], in_=evT_d[:]))
        n_load0 = SY.val

        @block.gpsimd
        def _(g):
            g.wait_ge(dg, G.val)
            g.wait_ge(dq, n_load0)
            for h in range(16):
                for qt in range(NT):
                    dst = RawAP(tensor=S_d.tensor,
                                offset=h * 2048 * SROW + qt * 128 * SROW + 384,
                                ap=[[SROW, 128], [1, 128]])
                    G.op(g.dma_start(out=dst, in_=negrow[:]))

        @block.scalar
        def _(scalar):
            scalar.wait_ge(dq, n_load0)
            S_.op(scalar.copy(out=evtb[:], in_=SCR[0:64, 0:256]))
            S_.op(scalar.mul(out=qnw8[:], in_=qnw[:], mul=0.125))

        def do_layernorm(xnT):
            tprev = T.n
            for i in range(DT):
                @block.scalar
                def _(scalar, i=i):
                    scalar.wait_ge(st_, T.n)
                    S_.op(scalar.copy(out=xnT[i][:], in_=xd[i][:]))
                    S_.op(scalar.activation(out=SCRB[:, 0:512], in_=xd[i][:, 0:512], func=AF.Square))
                    S_.op(scalar.activation(out=SCRB[:, 512:1024], in_=xd[i][:, 512:1024], func=AF.Square))
                sc1 = S_.n

                @block.tensor
                def _(tensor, i=i, sc1=sc1):
                    tensor.wait_ge(ss_, sc1)
                    for c in range(2):
                        T.op(tensor.matmul(out=(PA if c == 0 else PA)[0:1, c * 512:(c + 1) * 512],
                                           lhsT=ones1[:], rhs=xnT[i][:, c * 512:(c + 1) * 512],
                                           start=(i == 0), stop=(i == DT - 1)))
                        T.op(tensor.matmul(out=PC[0:1, c * 512:(c + 1) * 512],
                                           lhsT=ones1[:], rhs=SCRB[:, c * 512:(c + 1) * 512],
                                           start=(i == 0), stop=(i == DT - 1)))

                @block.scalar
                def _(scalar, i=i):
                    scalar.wait_ge(st_, T.n)
                    S_.op(scalar.activation(out=SCRB[:, 0:512], in_=xd[i][:, 1024:1536], func=AF.Square))
                    S_.op(scalar.activation(out=SCRB[:, 512:1024], in_=xd[i][:, 1536:2048], func=AF.Square))
                sc2 = S_.n

                @block.tensor
                def _(tensor, i=i, sc2=sc2):
                    tensor.wait_ge(ss_, sc2)
                    for c in range(2):
                        T.op(tensor.matmul(out=PB[0:1, c * 512:(c + 1) * 512],
                                           lhsT=ones1[:], rhs=xnT[i][:, 1024 + c * 512:1024 + (c + 1) * 512],
                                           start=(i == 0), stop=(i == DT - 1)))
                        T.op(tensor.matmul(out=(PD if c == 0 else PE_)[0:1, 0:512],
                                           lhsT=ones1[:], rhs=SCRB[:, c * 512:(c + 1) * 512],
                                           start=(i == 0), stop=(i == DT - 1)))
            t_stats = T.n

            @block.scalar
            def _(scalar):
                scalar.wait_ge(st_, t_stats)
                S_.op(scalar.copy(out=SCR[0:1, 3072:3584], in_=PD[0:1, 0:512]))
                S_.op(scalar.copy(out=SCR[0:1, 3584:4096], in_=PE_[0:1, 0:512]))
            for c in range(4):
                @block.scalar
                def _(scalar, c=c):
                    scalar.wait_ge(st_, t_stats)
                    bs = PA[0:1, c * 512:(c + 1) * 512] if c < 2 else PB[0:1, (c - 2) * 512:(c - 1) * 512]
                    S_.op(scalar.copy(out=SCR[0:1, 2048:2560], in_=bs))
                    if c < 2:
                        S_.op(scalar.copy(out=SCR[0:1, 2560:3072], in_=PC[0:1, c * 512:(c + 1) * 512]))
                sev = S_.n

                @block.tensor
                def _(tensor, c=c, sev=sev):
                    tensor.wait_ge(ss_, sev)
                    if c == 0:
                        tensor.wait_ge(sv_, V.n)
                    bqr = (SCR[0:1, 2560:3072] if c < 2 else
                           (SCR[0:1, 3072:3584] if c == 2 else SCR[0:1, 3584:4096]))
                    T.op(tensor.matmul(out=PD[:, 0:512], lhsT=onesr[:], rhs=SCR[0:1, 2048:2560],
                                       start=True, stop=True))
                    T.op(tensor.matmul(out=PE_[:, 0:512], lhsT=onesr[:], rhs=bqr,
                                       start=True, stop=True))
                tb = T.n

                @block.scalar
                def _(scalar, tb=tb, vv=V.n):
                    scalar.wait_ge(st_, tb)
                    scalar.wait_ge(sv_, vv)
                    S_.op(scalar.mul(out=SCR[:, 0:512], in_=PD[:, 0:512], mul=1.0 / 1024))
                    S_.op(scalar.mul(out=SCR[:, 512:1024], in_=PE_[:, 0:512], mul=1.0 / 1024))
                smu = S_.n

                @block.vector
                def _(vector, smu=smu):
                    vector.wait_ge(ss_, smu)
                    V.op(vector.tensor_tensor(out=SCR[:, 1024:1536], in0=SCR[:, 0:512],
                                              in1=SCR[:, 0:512], op=ALU.mult))
                    vector.wait_ge(sv_, V.n)
                    V.op(vector.tensor_tensor(out=SCR[:, 1536:2048], in0=SCR[:, 512:1024],
                                              in1=SCR[:, 1024:1536], op=ALU.subtract))
                vvar = V.n

                @block.scalar
                def _(scalar, vvar=vvar):
                    scalar.wait_ge(sv_, vvar)
                    S_.op(scalar.activation(out=SCR[:, 1536:2048], in_=SCR[:, 1536:2048],
                                            func=AF.Sqrt, bias=cvec[:, 0:1]))
                srs = S_.n

                @block.vector
                def _(vector, c=c, srs=srs):
                    vector.wait_ge(ss_, srs)
                    V.op(vector.reciprocal(out=SCR[:, 1536:2048], in_=SCR[:, 1536:2048]))
                    for i in range(DT):
                        vector.wait_ge(sv_, V.n)
                        V.op(vector.tensor_tensor(out=SCR[:, 1024:1536], in0=xd[i][:, c * 512:(c + 1) * 512],
                                                  in1=SCR[:, 0:512], op=ALU.subtract))
                        vector.wait_ge(sv_, V.n)
                        V.op(vector.tensor_tensor(out=xnT[i][:, c * 512:(c + 1) * 512],
                                                  in0=SCR[:, 1024:1536], in1=SCR[:, 1536:2048],
                                                  op=ALU.mult))
            return

        def fence():
            cs = (T.n, V.n, S_.n, SY.val, G.val)
            @block.tensor
            def _(tensor):
                tensor.wait_ge(sv_, cs[1]); tensor.wait_ge(ss_, cs[2])
                tensor.wait_ge(dq, cs[3]); tensor.wait_ge(dg, cs[4])
            @block.vector
            def _(vector):
                vector.wait_ge(st_, cs[0]); vector.wait_ge(ss_, cs[2])
                vector.wait_ge(dq, cs[3]); vector.wait_ge(dg, cs[4])
            @block.scalar
            def _(scalar):
                scalar.wait_ge(st_, cs[0]); scalar.wait_ge(sv_, cs[1])
                scalar.wait_ge(dq, cs[3]); scalar.wait_ge(dg, cs[4])
            @block.sync
            def _(sync):
                sync.wait_ge(dq, SY.val)
                sync.wait_ge(st_, cs[0]); sync.wait_ge(sv_, cs[1]); sync.wait_ge(ss_, cs[2])
                sync.wait_ge(dg, cs[4])
            @block.gpsimd
            def _(g):
                g.wait_ge(dg, G.val)
                g.wait_ge(st_, cs[0]); g.wait_ge(sv_, cs[1]); g.wait_ge(ss_, cs[2])
                g.wait_ge(dq, cs[3])

        def attn_heads(a, nb, Qt, Kt_of, Vt_of, par, nheads_kv, bias_kind,
                       Bd=None, Bm=None, qe0=None, wo_tiles=None, scale_first_kb=None):
            kbs = list(range(a - nb + 1, a + 1))
            t_tr_prev = [0]
            ps_cons = {}
            tr_done = 0
            for h in range(16):
                p = h % 2
                ps = PA if p == 0 else PB

                @block.tensor
                def _(tensor, h=h, p=p, ps=ps, vv=V.n):
                    if h < 2:
                        tensor.wait_ge(sv_, vv)
                    if h >= 2 and (h - 2) in ps_cons:
                        vc, sc = ps_cons[h - 2]
                        tensor.wait_ge(sv_, vc)
                        tensor.wait_ge(ss_, sc)
                    for j, kb in enumerate(kbs):
                        o = ps[:, j * 128:(j + 1) * 128]
                        qs = Qt[0:64, h * 128:(h + 1) * 128]
                        kt, koff = Kt_of(kb, h)
                        ks = kt[0:64, koff:koff + 128]
                        if bias_kind == 'a':
                            masked = (kb == a - 4) or (kb == a)
                            if kb == a - 4:
                                T.op(tensor.matmul(out=o, lhsT=identb[:], rhs=triu[:], start=True, stop=False))
                            elif kb == a:
                                T.op(tensor.matmul(out=o, lhsT=identb[:], rhs=tril[:], start=True, stop=False))
                            T.op(tensor.matmul(out=o, lhsT=qs, rhs=ks, start=(not masked), stop=True))
                        else:
                            delta = (a - kb) * 128
                            if delta == 256:
                                T.op(tensor.matmul(out=o, lhsT=identb[:], rhs=triu[:], start=True, stop=False))
                                T.op(tensor.matmul(out=o, lhsT=qs, rhs=ks, start=False, stop=True))
                            elif delta == 128:
                                T.op(tensor.matmul(out=o, lhsT=qs, rhs=ks, start=True, stop=False))
                                T.op(tensor.matmul(out=o, lhsT=identb[:], rhs=Bm[0:128, h * 128:(h + 1) * 128],
                                                   start=False, stop=True))
                            else:
                                T.op(tensor.matmul(out=o, lhsT=qs, rhs=ks, start=True, stop=False))
                                T.op(tensor.matmul(out=o, lhsT=identb[:], rhs=Bd[0:128, h * 128:(h + 1) * 128],
                                                   start=False, stop=True))
                t_sc = T.n

                @block.vector
                def _(vector, h=h, ps=ps, t_sc=t_sc):
                    vector.wait_ge(st_, t_sc)
                    if bias_kind == 'b' and kbs[0] == a - 2:
                        V.op(vector.tensor_scalar(out=ps[:, 0:128], in0=ps[:, 0:128],
                                                  scalar1=qe0[:, h:h + 1], scalar2=None, op0=ALU.add))
                        vector.wait_ge(sv_, V.n)
                    V.op(vector.reduce_max(out=nmax[:, h:h + 1], in_=ps[:, 0:nb * 128],
                                           axis=AX.X, negate=True))
                v_red = V.n

                @block.scalar
                def _(scalar, h=h, ps=ps, v_red=v_red, ttp=t_tr_prev[0]):
                    scalar.wait_ge(sv_, v_red)
                    scalar.wait_ge(st_, ttp)
                    S_.op(scalar.activation(out=expP[:, 0:nb * 128], in_=ps[:, 0:nb * 128], func=AF.Exp,
                                            bias=nmax[:, h:h + 1], accum_out=ztab[:, h:h + 1]))
                s_exp = S_.n
                ps_cons[h] = (v_red, s_exp)

                @block.vector
                def _(vector, h=h, s_exp=s_exp):
                    vector.wait_ge(ss_, s_exp)
                    V.op(vector.reciprocal(out=rztab[:, h:h + 1], in_=ztab[:, h:h + 1]))
                    vector.wait_ge(sv_, V.n)
                    V.op(vector.tensor_scalar(out=diagt[:], in0=identb[:],
                                              scalar1=rztab[:, h:h + 1], scalar2=None, op0=ALU.mult))
                v_diag = V.n

                @block.tensor
                def _(tensor, v_diag=v_diag, tr_done=tr_done):
                    tensor.wait_ge(sv_, v_diag)
                    for j in range(nb):
                        T.op(tensor.matmul(out=PC[:, j * 128:(j + 1) * 128],
                                           lhsT=expP[:, j * 128:(j + 1) * 128], rhs=diagt[:],
                                           start=True, stop=True))
                t_tr = T.n
                t_tr_prev[0] = t_tr

                @block.vector
                def _(vector, t_tr=t_tr):
                    vector.wait_ge(st_, t_tr)
                    V.op(vector.tensor_copy(out=expTb[:, 0:nb * 128], in_=PC[:, 0:nb * 128]))
                v_evt = V.n
                tr_done = v_evt

                @block.tensor
                def _(tensor, h=h, v_evt=v_evt):
                    tensor.wait_ge(sv_, v_evt)
                    pd = PD if (h % 2 == 0) else PE_
                    for j, kb in enumerate(kbs):
                        vt, vcol = Vt_of(kb, h)
                        T.op(tensor.matmul(out=pd[0:64, 0:128],
                                           lhsT=vt[:, vcol:vcol + 64],
                                           rhs=expTb[:, j * 128:(j + 1) * 128],
                                           start=(j == 0), stop=(j == nb - 1)))
                t_av = T.n

                @block.scalar
                def _(scalar, h=h, a=a, t_av=t_av, gv=G.val):
                    scalar.wait_ge(st_, t_av)
                    scalar.wait_ge(dg, gv)
                    pd = PD if (h % 2 == 0) else PE_
                    if h % 2 == 0:
                        S_.op(scalar.copy(out=YTb[a % 2][0:64, (h // 2) * 128:(h // 2) * 128 + 128],
                                          in_=pd[0:64, 0:128]))
                    else:
                        S_.op(scalar.copy(out=YTodd[0:64, (h // 2) * 128:(h // 2) * 128 + 128],
                                          in_=pd[0:64, 0:128]))
                s_y = S_.n

                if h % 2 == 1:
                    @block.gpsimd
                    def _(g, h=h, a=a, s_y=s_y):
                        g.wait_ge(dg, G.val)
                        g.wait_ge(ss_, s_y)
                        G.op(g.dma_start(out=YTb[a % 2][64:128, (h // 2) * 128:(h // 2) * 128 + 128],
                                         in_=YTodd[0:64, (h // 2) * 128:(h // 2) * 128 + 128]))
            g_yt = G.val

            for ot in range(DT):
                p = ot % 2
                ps = PA if p == 0 else PB

                @block.tensor
                def _(tensor, ot=ot, ps=ps, a=a, vv=V.n, ssn=S_.n):
                    if ot == 0:
                        tensor.wait_ge(sv_, vv)
                        tensor.wait_ge(ss_, ssn)
                        tensor.wait_ge(dg, g_yt)
                    if ot >= 2:
                        tensor.wait_ge(sv_, oproj_ev[ot - 2])
                    for ct in range(8):
                        T.op(tensor.matmul(out=ps[:, 0:128],
                                           lhsT=wo_tiles[ct][:, ot * 128:(ot + 1) * 128],
                                           rhs=YTb[a % 2][:, ct * 128:(ct + 1) * 128],
                                           start=(ct == 0), stop=(ct == 7)))
                t_op = T.n

                @block.vector
                def _(vector, ot=ot, ps=ps, a=a, t_op=t_op):
                    vector.wait_ge(st_, t_op)
                    V.op(vector.tensor_tensor(out=xd[ot][:, a * 128:(a + 1) * 128],
                                              in0=ps[:, 0:128],
                                              in1=xd[ot][:, a * 128:(a + 1) * 128], op=ALU.add))
                oproj_ev[ot] = V.n
            return

        oproj_ev = [0] * DT

        ph1 = ExitStack()
        xnT = [ph1.enter_context(nc.sbuf_tensor(f"xnT{i}", [128, 2048], BF16)) for i in range(DT)]
        do_layernorm(xnT)
        ln1_v = V.n

        WKV = [ph1.enter_context(nc.sbuf_tensor(f"WKV{i}", [128, 512], BF16)) for i in range(DT)]
        wstage = ph1.enter_context(nc.sbuf_tensor("wstage", [128, 1024], F32))
        KTa = ph1.enter_context(nc.sbuf_tensor("KTa", [64, 8192], BF16))
        Va = [ph1.enter_context(nc.sbuf_tensor(f"Va{i}", [128, 256], BF16)) for i in range(NT)]
        Kstg = ph1.enter_context(nc.sbuf_tensor("Kstg", [128, 512], BF16))
        Qstg = ph1.enter_context(nc.sbuf_tensor("Qstg", [128, 1024], BF16))
        Qac = [ph1.enter_context(nc.sbuf_tensor(f"Qac{i}", [64, 2048], BF16)) for i in range(2)]

        cast_done = []
        for i in range(DT):
            @block.sync
            def _(sync, i=i):
                sync.wait_ge(dq, SY.val)
                if i >= 2:
                    sync.wait_ge(ss_, cast_done[i - 2])
                if i == 2 or i == 3:
                    pass
                SY.op(sync.dma_start(out=wstage[:, (i % 2) * 512:(i % 2) * 512 + 256],
                                     in_=wkT_d[i * 128:(i + 1) * 128, :]))
                SY.op(sync.dma_start(out=wstage[:, (i % 2) * 512 + 256:(i % 2) * 512 + 512],
                                     in_=wvT_d[i * 128:(i + 1) * 128, :]))
            ld = SY.val

            @block.scalar
            def _(scalar, i=i, ld=ld):
                scalar.wait_ge(dq, ld)
                S_.op(scalar.copy(out=WKV[i][:], in_=wstage[:, (i % 2) * 512:(i % 2) * 512 + 512]))
            cast_done.append(S_.n)
        wkv_ready = S_.n

        for ot in range(2):
            for tc in range(4):
                @block.tensor
                def _(tensor, ot=ot, tc=tc, vv=V.n, ssn=S_.n, gv=G.val):
                    tensor.wait_ge(ss_, wkv_ready)
                    tensor.wait_ge(dg, gv)
                    if ot == 0 and tc == 0:
                        tensor.wait_ge(sv_, ln1_v)
                    else:
                        tensor.wait_ge(sv_, vv)
                        tensor.wait_ge(ss_, ssn)
                    for dt in range(DT):
                        T.op(tensor.matmul(out=PA[:, 0:512],
                                           lhsT=WKV[dt][:, ot * 128:ot * 128 + 128],
                                           rhs=xnT[dt][:, tc * 512:(tc + 1) * 512],
                                           start=(dt == 0), stop=(dt == DT - 1)))
                t_k = T.n

                @block.scalar
                def _(scalar, t_k=t_k):
                    scalar.wait_ge(st_, t_k)
                    S_.op(scalar.activation(out=SCRB[:, 0:512], in_=PA[:, 0:512], func=AF.Square))
                s_sq = S_.n

                @block.tensor
                def _(tensor, s_sq=s_sq):
                    tensor.wait_ge(ss_, s_sq)
                    T.op(tensor.matmul(out=PD[0:2, 0:512], lhsT=ones2[:], rhs=SCRB[:, 0:512],
                                       start=True, stop=True))
                t_ss = T.n

                @block.scalar
                def _(scalar, t_ss=t_ss):
                    scalar.wait_ge(st_, t_ss)
                    S_.op(scalar.activation(out=SCR[0:2, 512:1024], in_=PD[0:2, 0:512],
                                            func=AF.Sqrt, bias=cvec[0:2, 1:2]))
                s_rr = S_.n

                @block.vector
                def _(vector, s_rr=s_rr):
                    vector.wait_ge(ss_, s_rr)
                    V.op(vector.reciprocal(out=SCR[0:2, 512:1024], in_=SCR[0:2, 512:1024]))
                v_rr = V.n

                @block.tensor
                def _(tensor, v_rr=v_rr):
                    tensor.wait_ge(sv_, v_rr)
                    T.op(tensor.matmul(out=PE_[:, 0:512], lhsT=sel2[:], rhs=SCR[0:2, 512:1024],
                                       start=True, stop=True))
                t_bc = T.n

                @block.scalar
                def _(scalar, t_bc=t_bc):
                    scalar.wait_ge(st_, t_bc)
                    S_.op(scalar.copy(out=SCR[:, 1024:1536], in_=PA[:, 0:512]))
                s_kraw = S_.n

                @block.vector
                def _(vector, t_bc=t_bc, s_kraw=s_kraw):
                    vector.wait_ge(st_, t_bc)
                    vector.wait_ge(ss_, s_kraw)
                    V.op(vector.tensor_tensor(out=SCR[:, 0:512], in0=SCR[:, 1024:1536],
                                              in1=PE_[:, 0:512], op=ALU.mult))
                    vector.wait_ge(sv_, V.n)
                    V.op(vector.tensor_scalar(out=Kstg[:, 0:512], in0=SCR[:, 0:512],
                                              scalar1=knw[:, 0:1], scalar2=None, op0=ALU.mult))
                v_k = V.n

                @block.gpsimd
                def _(g, ot=ot, tc=tc, v_k=v_k):
                    g.wait_ge(dg, G.val)
                    g.wait_ge(sv_, v_k)
                    G.op(g.dma_start(out=KTa[0:64, (2 * ot) * 2048 + tc * 512:(2 * ot) * 2048 + (tc + 1) * 512],
                                     in_=Kstg[0:64, 0:512]))
                    G.op(g.dma_start(out=KTa[0:64, (2 * ot + 1) * 2048 + tc * 512:(2 * ot + 1) * 2048 + (tc + 1) * 512],
                                     in_=Kstg[64:128, 0:512]))
        g_ka = G.val

        for i in range(NT):
            @block.tensor
            def _(tensor, i=i, vv=V.n):
                tensor.wait_ge(sv_, vv)
                if i >= 2:
                    tensor.wait_ge(ss_, va_ev[i - 2])
                for dt in range(DT):
                    T.op(tensor.matmul(out=(PB if i % 2 == 0 else PC)[:, 0:256],
                                       lhsT=xnT[dt][:, i * 128:(i + 1) * 128],
                                       rhs=WKV[dt][:, 256:512],
                                       start=(dt == 0), stop=(dt == DT - 1)))
            t_v = T.n

            @block.scalar
            def _(scalar, i=i, t_v=t_v):
                scalar.wait_ge(st_, t_v)
                S_.op(scalar.copy(out=Va[i][:], in_=(PB if i % 2 == 0 else PC)[:, 0:256]))
            va_ev.append(S_.n)

        va_done = S_.n

        WQ = [ph1.enter_context(nc.sbuf_tensor(f"WQ{i}", [128, 1024], BF16)) for i in range(DT)]
        WO = [ph1.enter_context(nc.sbuf_tensor(f"WO{i}", [128, 1024], BF16)) for i in range(DT)]
        for i in range(DT):
            @block.sync
            def _(sync, i=i):
                sync.wait_ge(dq, SY.val)
                if i >= 1:
                    sync.wait_ge(ss_, qo_cast[i - 1])
                else:
                    sync.wait_ge(ss_, wkv_ready)
                SY.op(sync.dma_start(out=wstage[:, 0:512], in_=wqT_d[i * 128:(i + 1) * 128, 0:512]))
                SY.op(sync.dma_start(out=wstage[:, 512:1024], in_=wqT_d[i * 128:(i + 1) * 128, 512:1024]))
            ld1 = SY.val

            @block.scalar
            def _(scalar, i=i, ld1=ld1):
                scalar.wait_ge(dq, ld1)
                S_.op(scalar.copy(out=WQ[i][:], in_=wstage[:, 0:1024]))
            c1 = S_.n

            @block.sync
            def _(sync, i=i, c1=c1):
                sync.wait_ge(dq, SY.val)
                sync.wait_ge(ss_, c1)
                SY.op(sync.dma_start(out=wstage[:, 0:512], in_=woT_d[i * 128:(i + 1) * 128, 0:512]))
                SY.op(sync.dma_start(out=wstage[:, 512:1024], in_=woT_d[i * 128:(i + 1) * 128, 512:1024]))
            ld2 = SY.val

            @block.scalar
            def _(scalar, i=i, ld2=ld2):
                scalar.wait_ge(dq, ld2)
                S_.op(scalar.copy(out=WO[i][:], in_=wstage[:, 0:1024]))
            qo_cast.append(S_.n)
        wqo_ready = S_.n

        qac_ready = {}
        for step in range(NT + 1):
            if step < NT:
                i = step
                for ot in range(DT):
                    @block.tensor
                    def _(tensor, i=i, ot=ot, vv=V.n, ssn=S_.n):
                        tensor.wait_ge(ss_, max(wqo_ready, va_done, ssn))
                        tensor.wait_ge(sv_, vv)
                        for dt in range(DT):
                            T.op(tensor.matmul(out=PC[:, 768:896],
                                               lhsT=WQ[dt][:, ot * 128:(ot + 1) * 128],
                                               rhs=xnT[dt][:, i * 128:(i + 1) * 128],
                                               start=(dt == 0), stop=(dt == DT - 1)))
                    t_q = T.n

                    @block.scalar
                    def _(scalar, t_q=t_q):
                        scalar.wait_ge(st_, t_q)
                        S_.op(scalar.activation(out=SCRB[:, 0:128], in_=PC[:, 768:896], func=AF.Square))
                    s_sq = S_.n

                    @block.tensor
                    def _(tensor, s_sq=s_sq):
                        tensor.wait_ge(ss_, s_sq)
                        T.op(tensor.matmul(out=PD[0:2, 256:384], lhsT=ones2[:], rhs=SCRB[:, 0:128],
                                           start=True, stop=True))
                    t_ss = T.n

                    @block.scalar
                    def _(scalar, t_ss=t_ss):
                        scalar.wait_ge(st_, t_ss)
                        S_.op(scalar.activation(out=SCR[0:2, 2560:2688], in_=PD[0:2, 256:384],
                                                func=AF.Sqrt, bias=cvec[0:2, 1:2]))
                    s_rr = S_.n

                    @block.vector
                    def _(vector, s_rr=s_rr):
                        vector.wait_ge(ss_, s_rr)
                        V.op(vector.reciprocal(out=SCR[0:2, 2560:2688], in_=SCR[0:2, 2560:2688]))
                    v_rr = V.n

                    @block.tensor
                    def _(tensor, v_rr=v_rr):
                        tensor.wait_ge(sv_, v_rr)
                        T.op(tensor.matmul(out=PE_[:, 256:384], lhsT=sel2[:], rhs=SCR[0:2, 2560:2688],
                                           start=True, stop=True))
                    t_bc = T.n

                    @block.scalar
                    def _(scalar, t_bc=t_bc):
                        scalar.wait_ge(st_, t_bc)
                        S_.op(scalar.copy(out=SCR[:, 2816:2944], in_=PC[:, 768:896]))
                    s_qraw = S_.n

                    @block.vector
                    def _(vector, ot=ot, t_bc=t_bc, gv=G.val, s_qraw=s_qraw):
                        vector.wait_ge(st_, t_bc)
                        vector.wait_ge(ss_, s_qraw)
                        if ot == 0:
                            vector.wait_ge(dg, gv)
                        V.op(vector.tensor_tensor(out=SCR[:, 2688:2816], in0=SCR[:, 2816:2944],
                                                  in1=PE_[:, 256:384], op=ALU.mult))
                        vector.wait_ge(sv_, V.n)
                        V.op(vector.tensor_scalar(out=Qstg[:, ot * 128:(ot + 1) * 128],
                                                  in0=SCR[:, 2688:2816],
                                                  scalar1=qnw8[:, 0:1], scalar2=None, op0=ALU.mult))
                    q_ev = V.n

                @block.gpsimd
                def _(g, i=i, q_ev=q_ev):
                    g.wait_ge(dg, G.val)
                    g.wait_ge(sv_, q_ev)
                    for ot in range(DT):
                        G.op(g.dma_start(out=Qac[i % 2][0:64, (2 * ot) * 128:(2 * ot) * 128 + 128],
                                         in_=Qstg[0:64, ot * 128:(ot + 1) * 128]))
                        G.op(g.dma_start(out=Qac[i % 2][0:64, (2 * ot + 1) * 128:(2 * ot + 1) * 128 + 128],
                                         in_=Qstg[64:128, ot * 128:(ot + 1) * 128]))
                qac_ready[i] = G.val

            if step >= 1:
                a = step - 1
                nb = min(a + 1, 5)

                @block.tensor
                def _(tensor, a=a):
                    tensor.wait_ge(dg, max(qac_ready[a], g_ka))
                    tensor.wait_ge(ss_, va_done)

                attn_heads(a, nb, Qac[a % 2],
                           lambda kb, h: (KTa, (h // 4) * 2048 + kb * 128),
                           lambda kb, h: (Va[kb], (h // 4) * 64),
                           a % 2, 4, 'a', wo_tiles=WO)

        ph1.close()
        fence()

        def do_ffn(xnT, WF, first):
            wf32 = []
            phf = ExitStack()
            wf32 = [phf.enter_context(nc.sbuf_tensor(f"wf{first}_{i}", [128, 1024], F32)) for i in range(DT)]
            for i in range(DT):
                @block.sync
                def _(sync, i=i, tn=T.n, vn=V.n, sn=S_.n):
                    sync.wait_ge(dq, SY.val)
                    sync.wait_ge(st_, tn); sync.wait_ge(sv_, vn); sync.wait_ge(ss_, sn)
                    SY.op(sync.dma_start(out=wf32[i][:, 0:512], in_=wfT_d[i * 128:(i + 1) * 128, 0:512]))
                    SY.op(sync.dma_start(out=wf32[i][:, 512:1024], in_=wfT_d[i * 128:(i + 1) * 128, 512:1024]))
            wf_ld = SY.val

            @block.vector
            def _(vector):
                vector.wait_ge(dq, wf_ld)
                for i in range(DT):
                    V.op(vector.reduce_sum(out=SCR[:, 1024 + i:1025 + i], in_=wf32[i][:],
                                           axis=AX.X, apply_absolute_value=True))
                vector.wait_ge(sv_, V.n)
                V.op(vector.reduce_sum(out=SCR[:, 1032:1033], in_=SCR[:, 1024:1032], axis=AX.X))
            v_aw = V.n

            @block.tensor
            def _(tensor, v_aw=v_aw):
                tensor.wait_ge(sv_, v_aw)
                T.op(tensor.matmul(out=PD[0:1, 0:1], lhsT=SCR[:, 1032:1033], rhs=onesc[:],
                                   start=True, stop=True))
            t_sum = T.n

            @block.scalar
            def _(scalar, t_sum=t_sum):
                scalar.wait_ge(st_, t_sum)
                S_.op(scalar.copy(out=SCR[0:1, 1040:1041], in_=PD[0:1, 0:1]))
            s_c = S_.n

            @block.tensor
            def _(tensor, s_c=s_c):
                tensor.wait_ge(ss_, s_c)
                T.op(tensor.matmul(out=PE_[:, 0:1], lhsT=onesr[:], rhs=SCR[0:1, 1040:1041],
                                   start=True, stop=True))
            t_bc = T.n

            @block.scalar
            def _(scalar, t_bc=t_bc):
                scalar.wait_ge(st_, t_bc)
                S_.op(scalar.mul(out=colA[:, 0:1], in_=PE_[:, 0:1], mul=1.0 / (1024.0 * 1024.0)))
            s_sc = S_.n

            @block.vector
            def _(vector, s_sc=s_sc):
                vector.wait_ge(ss_, s_sc)
                V.op(vector.tensor_scalar(out=colA[:, 0:1], in0=colA[:, 0:1],
                                          scalar1=1e-5, scalar2=None, op0=ALU.max))
            v_sc = V.n

            @block.scalar
            def _(scalar, v_sc=v_sc):
                scalar.wait_ge(sv_, v_sc)
                S_.op(scalar.mul(out=colA[:, 1:2], in_=colA[:, 0:1], mul=0.5))
            s_thr = S_.n

            tern_done = []
            for i in range(DT):
                @block.scalar
                def _(scalar, i=i, s_thr=s_thr):
                    scalar.wait_ge(ss_, s_thr)
                    if i >= 1:
                        scalar.wait_ge(sv_, tern_done[i - 1])
                    S_.op(scalar.sign(out=SCRB[:, 0:1024], in_=wf32[i][:]))
                    S_.op(scalar.activation(out=SCR[:, 0:1024], in_=wf32[i][:], func=AF.Abs))
                s_t = S_.n

                @block.vector
                def _(vector, i=i, s_t=s_t):
                    vector.wait_ge(ss_, s_t)
                    V.op(vector.tensor_scalar(out=WF[i][:], in0=SCR[:, 0:1024],
                                              scalar1=colA[:, 1:2], scalar2=None, op0=ALU.is_gt))
                    vector.wait_ge(sv_, V.n)
                    V.op(vector.tensor_tensor(out=WF[i][:], in0=WF[i][:], in1=SCRB[:, 0:1024],
                                              op=ALU.mult))
                tern_done.append(V.n)
            phf.close()
            tern_v = V.n

            for ot in range(DT):
                for tc in range(2):
                    p = (ot * 2 + tc) % 2
                    ps = PA if p == 0 else PB

                    @block.tensor
                    def _(tensor, ot=ot, tc=tc, ps=ps, vv=V.n):
                        tensor.wait_ge(sv_, vv)
                        for half in range(2):
                            for dt in range(DT):
                                T.op(tensor.matmul(out=ps[:, half * 512:(half + 1) * 512],
                                                   lhsT=WF[dt][:, ot * 128:(ot + 1) * 128],
                                                   rhs=xnT[dt][:, tc * 1024 + half * 512:tc * 1024 + (half + 1) * 512],
                                                   start=(dt == 0), stop=(dt == DT - 1)))
                    t_f = T.n

                    @block.vector
                    def _(vector, ot=ot, tc=tc, ps=ps, t_f=t_f):
                        vector.wait_ge(st_, t_f)
                        V.op(vector.tensor_scalar(out=SCR[:, 0:1024], in0=ps[:, 0:1024],
                                                  scalar1=colA[:, 0:1], scalar2=None, op0=ALU.mult))
                        vector.wait_ge(sv_, V.n)
                        V.op(vector.tensor_tensor(out=xd[ot][:, tc * 1024:(tc + 1) * 1024],
                                                  in0=SCR[:, 0:1024],
                                                  in1=xd[ot][:, tc * 1024:(tc + 1) * 1024],
                                                  op=ALU.add))
                    ffn_ev.append(V.n)

        ph2 = ExitStack()
        xnT = [ph2.enter_context(nc.sbuf_tensor(f"xn2_{i}", [128, 2048], BF16)) for i in range(DT)]
        WF = [ph2.enter_context(nc.sbuf_tensor(f"WF{i}", [128, 1024], BF16)) for i in range(DT)]
        do_layernorm(xnT)
        ffn_ev = []
        do_ffn(xnT, WF, 1)
        ph2.close()
        fence()

        ph3 = ExitStack()
        xnT = [ph3.enter_context(nc.sbuf_tensor(f"xn3_{i}", [128, 2048], BF16)) for i in range(DT)]
        do_layernorm(xnT)
        ln3_v = V.n

        WB = [ph3.enter_context(nc.sbuf_tensor(f"WB{i}", [128, 3072], BF16)) for i in range(DT)]
        wstg2 = ph3.enter_context(nc.sbuf_tensor("wstg2", [128, 1024], F32))
        bcasts = []
        for i in range(DT):
            for j in range(3):
                @block.sync
                def _(sync, i=i, j=j):
                    sync.wait_ge(dq, SY.val)
                    if bcasts:
                        sync.wait_ge(ss_, bcasts[-1])
                    SY.op(sync.dma_start(out=wstg2[:, 0:512],
                                         in_=wqkvT_d[i * 128:(i + 1) * 128, j * 1024:j * 1024 + 512]))
                    SY.op(sync.dma_start(out=wstg2[:, 512:1024],
                                         in_=wqkvT_d[i * 128:(i + 1) * 128, j * 1024 + 512:(j + 1) * 1024]))
                ldb = SY.val

                @block.scalar
                def _(scalar, i=i, j=j, ldb=ldb):
                    scalar.wait_ge(dq, ldb)
                    S_.op(scalar.copy(out=WB[i][:, j * 1024:(j + 1) * 1024], in_=wstg2[:, 0:1024]))
                bcasts.append(S_.n)
        wb_ready = S_.n

        QKstg = ph3.enter_context(nc.sbuf_tensor("QKstg", [128, 1024], BF16))
        Vstg = ph3.enter_context(nc.sbuf_tensor("Vstg", [128, 1024], BF16))

        pp_ev = []
        for i in range(NT):
            for which, coff, dest in [(0, 0, Qh_d), (1, 1024, Kh_d)]:
                for ot in range(DT):
                    @block.tensor
                    def _(tensor, i=i, coff=coff, ot=ot, ssn=S_.n):
                        tensor.wait_ge(ss_, max(wb_ready, ssn))
                        tensor.wait_ge(sv_, ln3_v)
                        if pp_ev:
                            tensor.wait_ge(dg, pp_ev[-1])
                        for dt in range(DT):
                            T.op(tensor.matmul(out=(PA if ot % 2 == 0 else PB)[:, 0:128],
                                               lhsT=WB[dt][:, coff + ot * 128:coff + (ot + 1) * 128],
                                               rhs=xnT[dt][:, i * 128:(i + 1) * 128],
                                               start=(dt == 0), stop=(dt == DT - 1)))
                    t_p = T.n

                    @block.scalar
                    def _(scalar, which=which, ot=ot, t_p=t_p):
                        scalar.wait_ge(st_, t_p)
                        src = (PA if ot % 2 == 0 else PB)[:, 0:128]
                        if which == 0:
                            S_.op(scalar.mul(out=QKstg[:, ot * 128:(ot + 1) * 128], in_=src, mul=0.125))
                        else:
                            S_.op(scalar.copy(out=QKstg[:, ot * 128:(ot + 1) * 128], in_=src))
                s_p = S_.n

                @block.gpsimd
                def _(g, i=i, dest=dest, s_p=s_p):
                    g.wait_ge(dg, G.val)
                    g.wait_ge(ss_, s_p)
                    for ot in range(DT):
                        G.op(g.dma_start(out=dest[i, 0:64, (2 * ot) * 128:(2 * ot) * 128 + 128],
                                         in_=QKstg[0:64, ot * 128:(ot + 1) * 128]))
                        G.op(g.dma_start(out=dest[i, 0:64, (2 * ot + 1) * 128:(2 * ot + 1) * 128 + 128],
                                         in_=QKstg[64:128, ot * 128:(ot + 1) * 128]))
                pp_ev.append(G.val)

            @block.tensor
            def _(tensor, i=i, pe=(pp_ev[-1] if pp_ev else 0), ssn=S_.n):
                tensor.wait_ge(dg, pe)
                tensor.wait_ge(ss_, ssn)
                for half in range(2):
                    for dt in range(DT):
                        T.op(tensor.matmul(out=PC[:, half * 512:(half + 1) * 512],
                                           lhsT=xnT[dt][:, i * 128:(i + 1) * 128],
                                           rhs=WB[dt][:, 2048 + half * 512:2048 + (half + 1) * 512],
                                           start=(dt == 0), stop=(dt == DT - 1)))
            t_v = T.n

            @block.scalar
            def _(scalar, t_v=t_v):
                scalar.wait_ge(st_, t_v)
                S_.op(scalar.copy(out=Vstg[:], in_=PC[:, 0:1024]))
            s_v = S_.n

            @block.gpsimd
            def _(g, i=i, s_v=s_v):
                g.wait_ge(dg, G.val)
                g.wait_ge(ss_, s_v)
                G.op(g.dma_start(out=Vh_d[i, :, :], in_=Vstg[:]))
            pp_ev.append(G.val)
        pp_done = G.val

        ph3.close()
        fence()

        ph4 = ExitStack()
        WO2 = [ph4.enter_context(nc.sbuf_tensor(f"WO2_{i}", [128, 1024], BF16)) for i in range(DT)]
        wstg3 = ph4.enter_context(nc.sbuf_tensor("wstg3", [128, 1024], F32))
        Qr = [ph4.enter_context(nc.sbuf_tensor(f"Qr{i}", [64, 2048], BF16)) for i in range(4)]
        Kr = [ph4.enter_context(nc.sbuf_tensor(f"Kr{i}", [64, 2048], BF16)) for i in range(5)]
        Vr = [ph4.enter_context(nc.sbuf_tensor(f"Vr{i}", [128, 1024], BF16)) for i in range(5)]
        qe0r = [ph4.enter_context(nc.sbuf_tensor(f"qe0r{i}", [128, 16], F32)) for i in range(4)]
        Bd = [ph4.enter_context(nc.sbuf_tensor(f"Bd{i}", [128, 2048], BF16)) for i in range(2)]
        Bm = [ph4.enter_context(nc.sbuf_tensor(f"Bm{i}", [128, 2048], BF16)) for i in range(2)]

        wo2c = []
        for i in range(DT):
            @block.sync
            def _(sync, i=i):
                sync.wait_ge(dq, SY.val)
                if i >= 1:
                    sync.wait_ge(ss_, wo2c[i - 1])
                SY.op(sync.dma_start(out=wstg3[:, 0:512], in_=woutT_d[i * 128:(i + 1) * 128, 0:512]))
                SY.op(sync.dma_start(out=wstg3[:, 512:1024], in_=woutT_d[i * 128:(i + 1) * 128, 512:1024]))
            ldo = SY.val

            @block.scalar
            def _(scalar, i=i, ldo=ldo):
                scalar.wait_ge(dq, ldo)
                S_.op(scalar.copy(out=WO2[i][:], in_=wstg3[:, 0:1024]))
            wo2c.append(S_.n)
        wo2_ready = S_.n

        qld = {}
        kvld = {}
        bld = {}
        qed_done = {}
        attn_b_cons = {}
        for step in range(NT + 2):
            @block.sync
            def _(sync, step=step):
                sync.wait_ge(dq, SY.val)
                sync.wait_ge(dg, pp_done)
                if step < NT:
                    if step >= 4:
                        sync.wait_ge(st_, attn_b_cons.get(step - 4, 0))
                    SY.op(sync.dma_start(out=Qr[step % 4][:], in_=Qh_d[step, :, :]))
                    if step >= 3:
                        sync.wait_ge(st_, attn_b_cons.get(step - 3, 0))
                    SY.op(sync.dma_start(out=Kr[step % 5][:], in_=Kh_d[step, :, :]))
                    SY.op(sync.dma_start(out=Vr[step % 5][:], in_=Vh_d[step, :, :]))
                b = step - 1
                if 0 <= b < NT:
                    sync.wait_ge(dg, qed_done.get(b, 0))
                    if b >= 2:
                        sync.wait_ge(st_, attn_b_cons.get(b - 2, 0))
                    if b > 0:
                        src = RawAP(tensor=S_d.tensor, offset=b * 128 * SROW + 255,
                                    ap=[[SROW - 1, 128], [2048 * SROW, 16], [1, 128]])
                        SY.op(sync.dma_start(out=Bm[b % 2][:], in_=src))
                    src = RawAP(tensor=S_d.tensor, offset=b * 128 * SROW + 383,
                                ap=[[SROW - 1, 128], [2048 * SROW, 16], [1, 128]])
                    SY.op(sync.dma_start(out=Bd[b % 2][:], in_=src))
                bld[step - 1] = SY.val
                qld[step] = SY.val
                kvld[step] = SY.val

            if step < NT:
                i = step
                for h in range(16):
                    @block.tensor
                    def _(tensor, i=i, h=h, sp=(qed_ev[-1] if qed_ev else 0)):
                        tensor.wait_ge(dq, qld[i])
                        tensor.wait_ge(ss_, sp)
                        if h == 0 and i >= 2:
                            tensor.wait_ge(sv_, attn_b_vcons.get(i - 4, 0))
                        T.op(tensor.matmul(out=PC[:, 640:896],
                                           lhsT=Qr[i % 4][0:64, h * 128:(h + 1) * 128],
                                           rhs=evtb[:],
                                           start=True, stop=True))
                    t_e = T.n

                    @block.scalar
                    def _(scalar, i=i, h=h, t_e=t_e, gv=G.val):
                        scalar.wait_ge(st_, t_e)
                        scalar.wait_ge(dg, gv)
                        S_.op(scalar.copy(out=SCRB[:, (h % 2) * 256:(h % 2) * 256 + 256],
                                          in_=PC[:, 640:896]))
                        S_.op(scalar.copy(out=qe0r[i % 4][:, h:h + 1], in_=PC[:, 703:704]))
                    qed_ev.append(S_.n)

                    @block.gpsimd
                    def _(g, i=i, h=h, sp=qed_ev[-1]):
                        g.wait_ge(dg, G.val)
                        g.wait_ge(ss_, sp)
                        dst = RawAP(tensor=S_d.tensor,
                                    offset=h * 2048 * SROW + i * 128 * SROW + 128,
                                    ap=[[SROW, 128], [1, 256]])
                        G.op(g.dma_start(out=dst, in_=SCRB[:, (h % 2) * 256:(h % 2) * 256 + 256]))
                qed_done[i] = G.val

            if step >= 2:
                a = step - 2
                nb = min(a + 1, 3)

                @block.tensor
                def _(tensor, a=a):
                    tensor.wait_ge(dq, max(kvld[a], bld[a], qld[a]))
                    tensor.wait_ge(dg, qed_done[a])
                    tensor.wait_ge(ss_, wo2_ready)

                attn_heads(a, nb, Qr[a % 4],
                           lambda kb, h: (Kr[kb % 5], h * 128),
                           lambda kb, h: (Vr[kb % 5], h * 64),
                           a % 2, 16, 'b',
                           Bd=Bd[a % 2], Bm=Bm[a % 2], qe0=qe0r[a % 4], wo_tiles=WO2)
                attn_b_cons[a] = T.n
                attn_b_vcons[a] = V.n

        ph4.close()
        fence()

        ph5 = ExitStack()
        xnT = [ph5.enter_context(nc.sbuf_tensor(f"xn5_{i}", [128, 2048], BF16)) for i in range(DT)]
        WF2 = [ph5.enter_context(nc.sbuf_tensor(f"WF2_{i}", [128, 1024], BF16)) for i in range(DT)]
        do_layernorm(xnT)
        ffn_ev = []
        do_ffn(xnT, WF2, 2)
        ph5.close()
        fence()

        fin_v = V.n

        @block.gpsimd
        def _(g):
            g.wait_ge(dg, G.val)
            g.wait_ge(sv_, fin_v)
            g.wait_ge(ss_, S_.n)
            g.wait_ge(st_, T.n)
            for i in range(DT):
                G.op(g.dma_start(out=out_d[i * 128:(i + 1) * 128, :], in_=xd[i][:]))
            g.wait_ge(dg, G.val)
            g.wait_ge(dq, SY.val)

    return nc


va_ev = []
qo_cast = []
qed_ev = []
attn_b_vcons = {}


def kernel(**inputs):
    global va_ev, qo_cast, qed_ev, attn_b_vcons
    va_ev, qo_cast, qed_ev, attn_b_vcons = [], [], [], {}

    x = inputs["L0_attn_a_x"]
    B = x.shape[0]
    f32 = np.float32

    def bf(a):
        return np.ascontiguousarray(a.astype(BFD))

    ar = np.ascontiguousarray
    shared = {
        "wqT": ar(inputs["L0_attn_a_wq"].T).astype(f32),
        "wkT": ar(inputs["L0_attn_a_wk"].T).astype(f32),
        "wvT": ar(inputs["L0_attn_a_wv"].T).astype(f32),
        "woT": ar(inputs["L0_attn_a_wo"].T).astype(f32),
        "qnw": ar(np.tile(inputs["L0_attn_a_q_norm_w"], 2).reshape(128, 1)).astype(f32),
        "knw": ar(np.tile(inputs["L0_attn_a_k_norm_w"], 2).reshape(128, 1)).astype(f32),
        "wqkvT": ar(inputs["L0_attn_b_w_qkv"].T).astype(f32),
        "woutT": ar(inputs["L0_attn_b_w_out"].T).astype(f32),
        "wfT": ar(inputs["L0_ffn_weight"].T).astype(f32),
    }
    rel = inputs["L0_attn_b_rel_emb"]
    idx = [max(v - 191, 0) for v in range(256)]
    shared["evT"] = ar(rel[idx].T).astype(f32)

    r = np.arange(128)[:, None]
    c = np.arange(128)[None, :]
    shared["identb"] = bf(np.eye(128, dtype=f32))
    shared["triu"] = bf(np.where(r < c, 0.0, NEG).astype(f32))
    shared["tril"] = bf(np.where(r >= c, 0.0, NEG).astype(f32))
    shared["negrow"] = bf(np.full((128, 128), NEG, dtype=f32))
    shared["ones1"] = bf(np.ones((128, 1), dtype=f32))
    on2 = np.zeros((128, 2), dtype=f32)
    on2[0:64, 0] = 1.0 / 64
    on2[64:128, 1] = 1.0 / 64
    shared["ones2"] = bf(on2)
    s2 = np.zeros((2, 128), dtype=f32)
    s2[0, 0:64] = 1.0
    s2[1, 64:128] = 1.0
    shared["sel2"] = ar(s2)
    shared["onesr"] = np.ones((1, 128), dtype=f32)
    cv = np.zeros((128, 8), dtype=f32)
    cv[:, 0] = 1e-5
    cv[:, 1] = 1e-6
    shared["cvec"] = cv
    shared["onesc"] = np.ones((128, 1), dtype=f32)

    nc = bass.Bass("TRN2", target_bir_lowering=False)
    build(nc)

    in_maps = []
    for b in range(B):
        m = dict(shared)
        m["xT"] = ar(x[b].T).astype(f32)
        in_maps.append(m)

    res = run_bass_kernel_spmd(nc, in_maps, core_ids=list(range(B)))
    out = np.stack([ar(res.results[b]["out"].T) for b in range(B)], axis=0).astype(np.float32)
    return out
